# version 05
# baseline (speedup 1.0000x reference)
import sys
sys.path.insert(0, "/opt/trn_rl_repo")
import math
import types
from contextlib import ExitStack
import numpy as np
import ml_dtypes
import concourse.bass as bass
import concourse.mybir as mybir
from concourse.bass_utils import run_bass_kernel_spmd
from concourse import library_config as libcfg
from concourse.library_overlay import lower_extended_insts

F32 = mybir.dt.float32
BF16 = mybir.dt.bfloat16
I16 = mybir.dt.int16
I32 = mybir.dt.int32
U32 = mybir.dt.uint32
U8 = mybir.dt.uint8
ALU = mybir.AluOpType
ACT = mybir.ActivationFunctionType
AXL = mybir.AxisListType

T = 2048
D = 1024
E = 16
K = 256
NB = 16
STRIP_CAP = 48
COMP_W = 20
H = 8
J = 2
DH = 128
CH = 128
NCH = T // CH

EPS_DN = 1e-6
LN_EPS = 1e-5
N_BISECT = 16


def alibi_r():
    def pow2(m):
        start = 2 ** (-(2 ** (-(math.log2(m) - 3))))
        return [start * start ** i for i in range(m)]
    s = np.array(pow2(H), dtype=np.float32)
    return np.exp(-s).astype(np.float32)


class Emitter:

    DMA_SEMS = ("sd", "cd", "cw", "wd0", "wd1", "gd0", "gd1", "sct", "ad", "od", "kd", "dd0", "dd1")

    def __init__(self, nc):
        self.nc = nc
        self.streams = {k: [] for k in ("sync", "scalar", "vector", "tensor", "gpsimd")}
        self._sems = {}
        self._counts = {}

    def semh(self, name):
        if name not in self._sems:
            self._sems[name] = self.nc.semaphore(name).__enter__()
            self._counts[name] = 0
        return self._sems[name]

    def bump(self, name, by):
        self.semh(name)
        self._counts[name] += by
        return self._counts[name]

    def count(self, name):
        self.semh(name)
        return self._counts[name]

    def ap(self, eng, build, inc=None, waits=(), dr=False):
        for (s, v) in waits:
            self.semh(s)
        wl = [(self._sems[s], self._counts[s] if s in self.DMA_SEMS else v)
              for (s, v) in waits]
        wl = [(h, v) for (h, v) in wl if v > 0]
        semdata = None
        ret = None
        if inc is not None:
            s, by = inc
            ret = self.bump(s, by)
            semdata = (self._sems[s], by)

        def fn(engine, _build=build, _sd=semdata, _wl=wl, _dr=dr):
            if _dr:
                engine.drain()
            for (h, v) in _wl:
                engine.wait_ge(h, v)
            ins = _build(engine)
            if _sd is not None:
                ins.then_inc(_sd[0], _sd[1])
        self.streams[eng].append(fn)
        return ret

    def wait(self, eng, name, value):
        h = self.semh(name)
        self.streams[eng].append(lambda engine, _h=h, _v=value: engine.wait_ge(_h, _v))

    def install(self, block):
        for name in ("sync", "scalar", "vector", "tensor", "gpsimd"):
            stream = self.streams[name]

            def make(st):
                def body(engine):
                    for fn in st:
                        fn(engine)
                return body
            getattr(block, name)(make(stream))


def _host_const_inputs():
    c = {}
    c["c_scal"] = np.tile(np.array(SCALAR_CONSTS, np.float32)[None, :], (128, 1))
    p = np.arange(128)[:, None]
    f = np.arange(128)[None, :]
    c["c_iota_tok"] = ((p % 16) * 128 + f + 1).astype(np.int16)
    k = np.arange(128)
    LT = ((k[:, None] // 16 == k[None, :] // 16) & (k[:, None] < k[None, :])).astype(np.float32)
    c["c_lt16"] = LT.astype(ml_dtypes.bfloat16)
    G16 = (k[:, None] // 16 == k[None, :] // 16).astype(np.float32)
    c["c_g16"] = G16.astype(ml_dtypes.bfloat16)
    c["c_q8"] = (k[:, None] // 16 == np.arange(8)[None, :]).astype(np.float32)
    c["c_id128f"] = np.eye(128, dtype=np.float32)
    c["c_id128"] = np.eye(128, dtype=np.float32).astype(ml_dtypes.bfloat16)
    c["c_caus"] = (k[:, None] <= k[None, :]).astype(np.float32)
    c["c_ones_row"] = np.ones((1, 128), dtype=np.float32).astype(ml_dtypes.bfloat16)
    r = alibi_r()
    c["c_r_hj"] = np.repeat(r, J).reshape(E, 1).astype(np.float32)
    return c


def make_sb(nc, stack):
    def sb(name, shape, dt):
        h = stack.enter_context(nc.sbuf_tensor(name, shape, dt))
        return h[tuple(slice(0, s) for s in shape)]
    return sb


SCALAR_CONSTS = [LN_EPS, EPS_DN, 0.5, 1.0]


def host_const_inputs():
    pass


def alloc_consts(nc, stack, em):
    cvals = _host_const_inputs()
    sb = make_sb(nc, stack)
    cs = types.SimpleNamespace()
    dtmap = {"c_iota_tok": I16, "c_lt16": BF16, "c_g16": BF16, "c_q8": F32, "c_id128f": F32, "c_id128": BF16,
             "c_caus": F32, "c_ones_row": BF16,
             "c_r_hj": F32, "c_scal": F32}
    cs._inputs = cvals
    for name, arr in cvals.items():
        dram = nc.dram_tensor(name, list(arr.shape), dtmap[name], kind="ExternalInput").ap()
        t = sb(name + "_sb", list(arr.shape), dtmap[name])
        em.ap("sync", lambda e, t=t, dram=dram: e.dma_start(out=t, in_=dram), inc=("sd", 16))
        setattr(cs, name, t)
    cs.ready = ("sd", em.count("sd"))
    for i, v in enumerate(SCALAR_CONSTS):
        nc.const_aps.register(F32, float(v), cs.c_scal[:, i:i + 1])
    for eng in ("scalar", "vector", "tensor", "gpsimd"):
        em.wait(eng, "sd", em.count("sd"))
    return cs


def alloc_psum_banks(nc, stack, t_):
    pb = [stack.enter_context(nc.psum_tensor(f"pbank{i}", [128, 512], F32))[:, :]
          for i in range(8)]
    t_.pb = pb
    t_.ps_cd = [pb[0], pb[1]]
    t_.ps_ab = [pb[2][:, 0:256], pb[3][:, 0:256]]
    t_.psum_mrg = pb[4][0:8, :].rearrange("p (h f) -> p h f", h=2)
    t_.psum_mrg2 = pb[5][0:8, :].rearrange("p (hs f) -> p hs f", hs=4)
    t_.psum_offs = pb[6][:, 0:8]
    t_.psum_vT = pb[6][:, 8:40].rearrange("p (hs q) -> p hs q", hs=4)
    t_.ps_sc = pb[1][:, 0:128]
    t_.ps_kt = [pb[2][:, 0:128], pb[3][:, 0:128]]
    t_.ps_ktb = [pb[2].bitcast(BF16)[:, 0:128], pb[3].bitcast(BF16)[:, 0:128]]
    t_.ps_A = pb[4][:, 0:258].rearrange("p (j x) -> p j x", j=2)
    t_.ps_B = pb[5][:, 0:256].rearrange("p (j x) -> p j x", j=2)
    t_.ps_S = pb[6][:, 0:258].rearrange("p (j x) -> p j x", j=2)
    t_.ps_Bz = pb[5][:, 0:258]
    return t_


def alloc_selection(nc, stack, cs, pall):
    t_ = types.SimpleNamespace()
    t_.pall = pall
    sb = make_sb(nc, stack)
    alloc_psum_banks(nc, stack, t_)
    t_.dr_tm = nc.dram_tensor("sel_dr_tm", [3, 128, 2, 128], F32, kind="Internal").ap()
    t_.dr_mrg = nc.dram_tensor("sel_dr_mrg", [16, 256], I16, kind="Internal").ap()
    t_.bis_lo = sb("bis_lo", [16, 1], F32)
    t_.bis_hi = sb("bis_hi", [16, 1], F32)
    t_.bis_t = sb("bis_t", [16, 1], F32)
    t_.bis_cnt = sb("bis_cnt", [16, 1], F32)
    t_.bis_pick = sb("bis_pick", [16, 1], F32)
    t_.bis_d = sb("bis_d", [16, 1], F32)
    t_.zeros_scan = sb("zeros_scan", [128, 2, 128], F32)
    t_.bis_scratch = None
    t_.flags_fm = sb("flags_fm", [16, T], F32)
    t_.bis_scratch = t_.flags_fm
    t_.fhi_tm = sb("fhi_tm", [128, 2, 128], F32)
    t_.gap_tm = sb("gap_tm", [128, 2, 128], F32)
    t_.cnthi_f = sb("cnthi_f", [128, 2], F32)
    t_.cnthi_bf = sb("cnthi_bf", [128, 2], BF16)
    t_.chibc = sb("chibc", [128, 2], F32)
    t_.probs_tm = sb("probs_tm", [128, 2, 128], F32)
    t_.flags_tm = sb("flags_tm", [128, 2, 128], F32)
    t_.incl = sb("incl", [128, 2, 128], F32)
    t_.valid = sb("valid", [128, 2, 128], F32)
    t_.tmp_tm = sb("tmp_tm", [128, 2, 128], F32)
    t_.cnt_bf = sb("cnt_bf", [128, 2], BF16)
    t_.cnt_f = sb("cnt_f", [128, 2], F32)
    t_.offs = sb("offs", [128, 2], F32)
    t_.pv_bf = sb("pv_bf", [128, 2, 128], BF16)
    t_.idx_ls = sb("idx_ls", [128, 2, 128], I16)
    t_.offs_fin = sb("offs_fin", [128, 2], F32)
    t_.cprime = sb("cprime", [128, 2], F32)
    t_.scat_idx = sb("scat_idx", [128, 2, K], I16)
    t_.scat_val = sb("scat_val", [128, 2, K], I16)
    t_.scat_f = None
    t_.merged_i16 = sb("merged_i16", [8, 2, K], I16)
    t_.valqp_f = sb("valqp_f", [8, 4, 128], F32)
    t_.vals_i16tm = sb("vals_i16tm", [128, 4, 8], I16)

    t_.nf = sb("nf", [128, 16], U32)
    t_.idx_all = sb("idx_all", [128, 16 * E], I16)
    t_.vals_tm = sb("vals_tm", [128, E, 2], F32)
    t_.idx_ls2 = sb("idx_ls2", [128, 2, 128], I16)
    t_.mi32 = None
    t_.mmod32 = None

    t_.zeros_v = sb("zeros_v", [128, 2, E], F32)
    t_.mi32 = t_.incl.bitcast(I32)
    t_.mmod32 = t_.gap_tm.bitcast(I32)
    t_.c_iota_tok = cs.c_iota_tok
    t_.c_lt16 = cs.c_lt16
    t_.c_g16 = cs.c_g16
    t_.c_q8 = cs.c_q8
    t_.c_id128f = cs.c_id128f
    t_.c_id128 = cs.c_id128
    t_.c_caus = cs.c_caus
    t_.c_ones_row = cs.c_ones_row
    t_.c_r_hj = cs.c_r_hj
    t_.c_scal = cs.c_scal
    return t_


def emit_selection(em, t_, probs_fm, probs_ready):
    V, S, G, TE = "vector", "scalar", "gpsimd", "tensor"
    _orig_ap = em.ap

    def _apv(eng, build, inc=None, waits=(), dr=None):
        return _orig_ap(eng, build, inc=inc, waits=waits,
                        dr=(eng in ("vector", "scalar") if dr is None else dr))
    em_ap_sel = _apv

    em_ap_sel(V, lambda e: e.memset(t_.bis_lo[:, :], 0.0), waits=probs_ready)
    em_ap_sel(V, lambda e: e.memset(t_.bis_hi[:, :], (1.0 / 256.0) * 1.001))
    em_ap_sel(V, lambda e: e.memset(t_.zeros_v[:, :, :], 0.0))
    em_ap_sel(V, lambda e: e.memset(t_.idx_all[:, :], 0))
    em_ap_sel(V, lambda e: e.memset(t_.nf[:, :], 0))
    for _ in range(N_BISECT):
        em_ap_sel(V, lambda e: e.tensor_add(t_.bis_t[:, :], t_.bis_lo[:, :], t_.bis_hi[:, :]))
        em_ap_sel(V, lambda e: e.tensor_scalar_mul(t_.bis_t[:, :], t_.bis_t[:, :], 0.5))
        em_ap_sel(V, lambda e: e.tensor_scalar(
            t_.bis_scratch[:, :], probs_fm, t_.bis_t[:, 0:1], None, ALU.is_ge))
        em_ap_sel(V, lambda e: e.reduce_sum(t_.bis_cnt[:, :], t_.bis_scratch[:, :], axis=AXL.X))
        em_ap_sel(V, lambda e: e.tensor_single_scalar(t_.bis_pick[:, :], t_.bis_cnt[:, :], 256.0, ALU.is_ge))
        em_ap_sel(V, lambda e: e.tensor_sub(t_.bis_d[:, :], t_.bis_t[:, :], t_.bis_lo[:, :]))
        em_ap_sel(V, lambda e: e.scalar_tensor_tensor(
            t_.bis_lo[:, :], t_.bis_d[:, :], t_.bis_pick[:, 0:1], t_.bis_lo[:, :], ALU.mult, ALU.add))
        em_ap_sel(V, lambda e: e.tensor_sub(t_.bis_d[:, :], t_.bis_hi[:, :], t_.bis_t[:, :]))
        em_ap_sel(V, lambda e: e.scalar_tensor_tensor(
            t_.bis_hi[:, :], t_.bis_d[:, :], t_.bis_pick[:, 0:1], t_.bis_t[:, :], ALU.mult, ALU.add))
    vflags = em_ap_sel(V, lambda e: e.tensor_scalar(
        t_.flags_fm[:, :], probs_fm, t_.bis_lo[:, 0:1], None, ALU.is_ge), inc=("v", 1))

    def bounce_wr(i, fmten, wv):
        wr = None
        for h in range(2):
            def b(e, i=i, h=h, fmten=fmten):
                dst = t_.dr_tm[i:i + 1, :, h:h + 1, :].rearrange(
                    "one (q k) one2 f -> (one q) k (one2 f)", k=16)
                src = fmten[8 * h:8 * h + 8, :].rearrange(
                    "q (k f) -> q k f", f=128)
                return e.dma_start(out=dst, in_=src)
            wr = em_ap_sel("sync", b, inc=("sd", 16), waits=[("v", wv)])
        return wr
    wr = bounce_wr(0, t_.flags_fm, vflags)
    bounce_wr(1, probs_fm, vflags)
    vfhi = em_ap_sel(V, lambda e: e.tensor_scalar(
        t_.flags_fm[:, :], probs_fm, t_.bis_hi[:, 0:1], None, ALU.is_ge),
        inc=("v", 1), waits=[("sd", wr)])
    wr = bounce_wr(2, t_.flags_fm, vfhi)
    dma_f = em_ap_sel("sync", lambda e: e.dma_start(
        out=t_.flags_tm[:, :, :], in_=t_.dr_tm[0:1, :, :, :].squeeze()),
        inc=("sd", 16), waits=[("sd", wr)])
    dma_p = em_ap_sel("sync", lambda e: e.dma_start(
        out=t_.probs_tm[:, :, :], in_=t_.dr_tm[1:2, :, :, :].squeeze()),
        inc=("sd", 16), waits=[("sd", wr)])
    dma_fh = em_ap_sel("sync", lambda e: e.dma_start(
        out=t_.fhi_tm[:, :, :], in_=t_.dr_tm[2:3, :, :, :].squeeze()),
        inc=("sd", 16), waits=[("sd", wr)])

    em_ap_sel(V, lambda e: e.tensor_sub(t_.gap_tm[:, :, :], t_.flags_tm[:, :, :], t_.fhi_tm[:, :, :]),
          waits=[("sd", dma_f), ("sd", dma_fh)])
    em_ap_sel(V, lambda e: e.reduce_sum(t_.cnt_f[:, :], t_.gap_tm[:, :, :], axis=AXL.X))
    em_ap_sel(V, lambda e: e.tensor_copy(t_.cnt_bf[:, :], t_.cnt_f[:, :]))
    em_ap_sel(V, lambda e: e.reduce_sum(t_.cnthi_f[:, :], t_.fhi_tm[:, :, :], axis=AXL.X))
    vcnt = em_ap_sel(V, lambda e: e.tensor_copy(t_.cnthi_bf[:, :], t_.cnthi_f[:, :]), inc=("v", 1))
    tofs = em_ap_sel(TE, lambda e: e.matmul(t_.psum_offs[:, 0:2], t_.c_lt16[:, :], t_.cnt_bf[:, :],
                                        start=True, stop=True),
                 inc=("t", 1), waits=[("v", vcnt)])
    tofs = em_ap_sel(TE, lambda e: e.matmul(t_.psum_offs[:, 2:4], t_.c_g16[:, :], t_.cnthi_bf[:, :],
                                        start=True, stop=True), inc=("t", 1))
    em_ap_sel(V, lambda e: e.tensor_copy(t_.offs[:, :], t_.psum_offs[:, 0:2]), waits=[("t", tofs)])
    em_ap_sel(V, lambda e: e.tensor_copy(t_.chibc[:, :], t_.psum_offs[:, 2:4]))

    em_ap_sel(V, lambda e: e.memset(t_.zeros_scan[:, :, :], 0.0))
    for h in range(2):
        em_ap_sel(V, lambda e, h=h: e.tensor_tensor_scan(
            t_.incl[:, h, :], t_.gap_tm[:, h, :], t_.zeros_scan[:, h, :], 0.0,
            ALU.add, ALU.add))
    for h in range(2):
        em_ap_sel(V, lambda e, h=h: e.tensor_scalar(
            t_.tmp_tm[:, h, :], t_.incl[:, h, :], t_.offs[:, h:h + 1], t_.chibc[:, h:h + 1],
            ALU.add, op1=ALU.add))
    em_ap_sel(V, lambda e: e.tensor_single_scalar(t_.valid[:, :, :], t_.tmp_tm[:, :, :], 256.0, ALU.is_le))
    em_ap_sel(V, lambda e: e.tensor_mul(t_.valid[:, :, :], t_.valid[:, :, :], t_.gap_tm[:, :, :]))
    em_ap_sel(V, lambda e: e.tensor_add(t_.valid[:, :, :], t_.valid[:, :, :], t_.fhi_tm[:, :, :]))
    for h in range(2):
        em_ap_sel(V, lambda e, h=h: e.tensor_tensor_scan(
            t_.incl[:, h, :], t_.valid[:, h, :], t_.zeros_scan[:, h, :], 0.0,
            ALU.add, ALU.add))
    em_ap_sel(V, lambda e: e.reduce_sum(t_.cnt_f[:, :], t_.valid[:, :, :], axis=AXL.X))
    vcnt2 = em_ap_sel(V, lambda e: e.tensor_copy(t_.cnt_bf[:, :], t_.cnt_f[:, :]), inc=("v", 1))
    tofs2 = em_ap_sel(TE, lambda e: e.matmul(t_.psum_offs[:, 4:6], t_.c_lt16[:, :], t_.cnt_bf[:, :],
                                             start=True, stop=True),
                      inc=("t", 1), waits=[("v", vcnt2)])
    tofs2 = em_ap_sel(TE, lambda e: e.matmul(t_.psum_offs[:, 6:8], t_.c_g16[:, :], t_.cnt_bf[:, :],
                                             start=True, stop=True), inc=("t", 1))
    em_ap_sel(V, lambda e: e.tensor_copy(t_.offs_fin[:, :], t_.psum_offs[:, 4:6]), waits=[("t", tofs2)])
    em_ap_sel(V, lambda e: e.tensor_copy(t_.cprime[:, :], t_.psum_offs[:, 6:8]))
    vnf = em_ap_sel(V, lambda e: e.tensor_copy(t_.nf[:, 0:2], t_.cprime[:, :]), inc=("v", 1))

    for h in range(2):
        em_ap_sel(V, lambda e, h=h: e.tensor_scalar(
            t_.tmp_tm[:, h, :], t_.incl[:, h, :], t_.offs_fin[:, h:h + 1], None, ALU.add))
    em_ap_sel(V, lambda e: e.tensor_mul(t_.tmp_tm[:, :, :], t_.tmp_tm[:, :, :], t_.valid[:, :, :]))
    em_ap_sel(V, lambda e: e.tensor_copy(t_.mi32[:, :, :], t_.tmp_tm[:, :, :]))
    em_ap_sel(V, lambda e: e.tensor_single_scalar(t_.mi32[:, :, :], t_.mi32[:, :, :], 1, ALU.subtract))
    em_ap_sel(V, lambda e: e.tensor_single_scalar(t_.mmod32[:, :, :], t_.mi32[:, :, :], 15, ALU.bitwise_and))
    em_ap_sel(V, lambda e: e.tensor_single_scalar(t_.mi32[:, :, :], t_.mi32[:, :, :], 4, ALU.arith_shift_right))
    em_ap_sel(V, lambda e: e.tensor_single_scalar(t_.mmod32[:, :, :], t_.mmod32[:, :, :], 4, ALU.logical_shift_left))
    em_ap_sel(V, lambda e: e.tensor_tensor(t_.mi32[:, :, :], t_.mmod32[:, :, :], t_.mi32[:, :, :], ALU.bitwise_or))
    em_ap_sel(V, lambda e: e.tensor_copy(t_.idx_ls[:, :, :], t_.mi32[:, :, :]))
    em_ap_sel(V, lambda e: e.tensor_single_scalar(t_.valid[:, :, :], t_.tmp_tm[:, :, :], 129.0, ALU.is_ge))
    em_ap_sel(V, lambda e: e.tensor_scalar(
        t_.tmp_tm[:, :, :], t_.tmp_tm[:, :, :], 2.0, -2.0, ALU.mult, op1=ALU.add))
    vidx = em_ap_sel(V, lambda e: e.scalar_tensor_tensor(
        t_.idx_ls2[:, :, :], t_.valid[:, :, :], -255.0, t_.tmp_tm[:, :, :], ALU.mult, ALU.add),
        inc=("v", 1))

    vpack = em_ap_sel(V, lambda e: e.tensor_copy(t_.pv_bf[:, :, :], t_.probs_tm[:, :, :]),
                      inc=("v", 1), waits=[("sd", dma_p)])

    em_ap_sel(G, lambda e: e.load_library(libcfg.local_scatter))
    gls = None
    for h in range(2):
        em_ap_sel(G, lambda e, h=h: e.local_scatter(
            t_.scat_idx[:, h, :],
            t_.c_iota_tok.unsqueeze(1).broadcast_to([128, 2, 128])[:, h, :],
            t_.idx_ls[:, h, :],
            128, K, 128), inc=("g", 1), waits=[("v", vpack), ("v", vidx)])
        gls = em_ap_sel(G, lambda e, h=h: e.local_scatter(
            t_.scat_val[:, h, :],
            t_.pv_bf[:, h, :].bitcast(I16),
            t_.idx_ls2[:, h, :],
            128, K, 128), inc=("g", 1))
    vsc = em_ap_sel(V, lambda e: e.tensor_copy(t_.scat_f[:, 0:2, :], t_.scat_idx[:, :, :]),
                    waits=[("g", gls)])
    vsc = em_ap_sel(V, lambda e: e.tensor_copy(t_.scat_f[:, 2:4, :], t_.scat_val[:, :, :]),
                    inc=("v", 1))
    tm = None
    for h in range(2):
        tm = em_ap_sel(TE, lambda e, h=h: e.matmul(
            t_.psum_mrg[:, h, :], t_.c_q8[:, :], t_.scat_f[:, h, :],
            start=True, stop=True), inc=("t", 1), waits=[("v", vsc)])
    vmg = em_ap_sel(V, lambda e: e.tensor_copy(t_.merged_i16[:, :, :], t_.psum_mrg[:, :, :]),
                    inc=("v", 1), waits=[("t", tm)])
    tv = None
    for h in range(2):
        for s in range(2):
            tv = em_ap_sel(TE, lambda e, h=h, s=s: e.matmul(
                t_.psum_mrg2[:, 2 * h + s, :], t_.c_q8[:, :],
                t_.scat_f[:, 2 + h, :].rearrange("p (pp two) -> p pp two", two=2)[:, :, s:s + 1].rearrange("p pp one -> p (pp one)"),
                start=True, stop=True), inc=("t", 1), waits=[("v", vsc)])
    vmg2 = em_ap_sel(V, lambda e: e.tensor_copy(t_.valqp_f[:, :, :], t_.psum_mrg2[:, :, :]),
                     inc=("v", 1), waits=[("t", tv)])
    tvt = None
    for h in range(2):
        for s in range(2):
            tvt = em_ap_sel(TE, lambda e, h=h, s=s: e.transpose(
                t_.psum_vT[:, 2 * h + s, :], t_.valqp_f[0:8, 2 * h + s, :], t_.c_id128f[0:8, 0:8]),
                inc=("t", 1), waits=[("v", vmg2)])
    vvv = em_ap_sel(V, lambda e: e.tensor_copy(t_.vals_i16tm[:, :, :], t_.psum_vT[:, :, :]),
                    inc=("v", 1), waits=[("t", tvt)])
    vvals = em_ap_sel(S, lambda e: e.activation(
        t_.vals_tm.rearrange("p (h q) s -> p h q s", h=2),
        t_.vals_i16tm.rearrange("p (h s) q -> p h q s", h=2).bitcast(BF16),
        ACT.Copy, scale=0.5),
        inc=("v", 1), waits=[("v", vvv)])

    dmw = em_ap_sel("sync", lambda e: e.dma_start(
        out=t_.dr_mrg.rearrange("(h q) j -> q h j", h=2),
        in_=t_.merged_i16[:, :, :]),
        inc=("sd", 16), waits=[("v", vmg)])
    for rep in range(2):
        dmi = em_ap_sel("sync", lambda e, rep=rep: e.dma_start(
            out=t_.idx_all[16 * rep:16 * rep + 16, :].rearrange("r (q c) -> r q c", c=16),
            in_=t_.dr_mrg.rearrange("q (r c) -> r q c", r=16)),
            inc=("sd", 16), waits=[("sd", dmw)])
    em_ap_sel(V, lambda e: e.tensor_single_scalar(
        t_.idx_all[0:32, :], t_.idx_all[0:32, :], 1, ALU.subtract), waits=[("sd", dmi)])
    vidxall = em_ap_sel(V, lambda e: e.tensor_single_scalar(
        t_.idx_all[0:32, :], t_.idx_all[0:32, :], -1, ALU.max), inc=("v", 1))
    return dict(vvals=vvals, vidxall=vidxall)


def run_sim(nc, in_map, out_names):
    from concourse.bass_interp import CoreSim
    sim = CoreSim(nc, trace=False, require_finite=False, require_nnan=False)
    for k, v in in_map.items():
        sim.tensor(k)[:] = v
    sim.simulate(check_with_hw=False)
    return {n: np.array(sim.tensor(n)) for n in out_names}



def alloc_ffn(nc, stack, cs, t_):
    sb = make_sb(nc, stack)
    t_.lnx_dram = nc.dram_tensor("lnx_dram", [T, D], BF16, kind="Internal").ap()
    t_.ln_bf = sb("ln_bf", [128, 4, D], BF16)
    t_.ln_stats = sb("ln_stats", [128, NB, 12], F32)
    t_.ln_mv = sb("ln_mv", [128, NB, 2], F32)
    t_.ln_rstd = sb("ln_rstd", [128, NB, 2], F32)
    t_.ln_fm = [sb(f"ln_fm{p}", [128, 8, 512], BF16) for p in range(2)]
    t_.wg_sb = sb("wg_sb", [128, 8, E], BF16)
    t_.bg_sb = sb("bg_sb", [16, 1], F32)
    t_.probs_fm = sb("probs_fm", [16, T], F32)
    t_.smx_m = sb("smx_m", [16, 1], F32)
    t_.smx_s = sb("smx_s", [16, 1], F32)
    t_.wbuf = [sb(f"wbuf{p}", [128, 8, D], BF16) for p in range(2)]
    t_.tokbuf = [sb(f"tokbuf{p}", [128, 8, K], BF16) for p in range(2)]
    t_.h1 = sb("h1", [128, 8, K], BF16)
    t_.h2 = sb("h2", [128, 8, K], BF16)
    t_.scat_src = [sb(f"scat_src{p}", [128, 2, D], F32) for p in range(2)]
    t_.bias_bc = sb("bias_bc", [128, 3, E], F32)
    t_.bias_row = sb("bias_row", [1, 3 * E], F32)
    return t_


def emit_ln(em, t_, resid_pair, out_ready_sem, ready=(), blocks=range(NB)):
    V, S = "vector", "scalar"
    last_dma = None
    for i, blk in enumerate(blocks):
        rt = resid_pair[blk % 2]
        g = blk // 2
        src = rt[:, g, :]
        slot = i % 4
        em.ap(V, lambda e, src=src, blk=blk: e.bn_stats(t_.ln_stats[:, blk, 0:6], src[:, 0:512]),
              waits=(list(ready) if i == 0 else []))
        em.ap(V, lambda e, src=src, blk=blk: e.bn_stats(t_.ln_stats[:, blk, 6:12], src[:, 512:1024]))
        va = em.ap(V, lambda e, blk=blk: e.bn_aggr(t_.ln_mv[:, blk, :], t_.ln_stats[:, blk, :]),
                   inc=("v", 1), dr=True)
        sv = em.ap(S, lambda e, blk=blk: e.activation(
            t_.ln_rstd[:, blk, 0:1], t_.ln_mv[:, blk, 1:2], ACT.Sqrt, bias=LN_EPS),
            inc=("s", 1), waits=[("v", va)])
        em.ap(V, lambda e, blk=blk: e.reciprocal(
            t_.ln_rstd[:, blk, 0:1], t_.ln_rstd[:, blk, 0:1]),
            waits=[("s", sv)], dr=True)
        em.ap(V, lambda e, blk=blk: e.tensor_mul(
            t_.ln_rstd[:, blk, 1:2], t_.ln_mv[:, blk, 0:1], t_.ln_rstd[:, blk, 0:1]), dr=True)
        vr = em.ap(V, lambda e, blk=blk: e.tensor_scalar_mul(
            t_.ln_rstd[:, blk, 1:2], t_.ln_rstd[:, blk, 1:2], -1.0), inc=("v", 1), dr=True)
        w = [("v", vr)]
        if i >= 4:
            w.append((out_ready_sem, 0))
        vn = em.ap(S, lambda e, src=src, blk=blk, slot=slot: e.activation(
            t_.ln_bf[:, slot, :], src, ACT.Identity,
            bias=t_.ln_rstd[:, blk, 1:2], scale=t_.ln_rstd[:, blk, 0:1]),
            inc=("s", 1), waits=w, dr=True)
        last_dma = em.ap("sync", lambda e, blk=blk, slot=slot: e.dma_start(
            out=t_.lnx_dram[128 * blk:128 * (blk + 1), :], in_=t_.ln_bf[:, slot, :]),
            inc=(out_ready_sem, 16), waits=[("s", vn)])
    return (out_ready_sem, last_dma)


def emit_gating(em, t_, pall, lnx_ready):
    V, S, TE = "vector", "scalar", "tensor"
    ps = t_.ps_cd[0][0:16, :]
    vprev = em.count("v")
    ld = {}
    for sl in range(4):
        par = sl % 2
        if sl < 2:
            for c in range(8):
                ld[sl] = em.ap("scalar", lambda e, sl=sl, c=c, par=par: e.dma_start(
                    out=t_.ln_fm[par][:, c, :],
                    in_=t_.lnx_dram[512 * sl:512 * (sl + 1), 128 * c:128 * (c + 1)],
                    transpose=True), inc=("ad", 16), waits=[lnx_ready])
        tmm = None
        for kc in range(8):
            tmm = em.ap(TE, lambda e, sl=sl, kc=kc, par=par: e.matmul(
                ps, t_.wg_sb[:, kc, :], t_.ln_fm[par][:, kc, :],
                start=(kc == 0), stop=(kc == 7)),
                inc=("t", 1),
                waits=([("ad", ld[sl]), ("cd", 1), ("cw", 0), ("v", vprev)] if kc == 0 else []))
        if sl + 2 < 4:
            for c in range(8):
                ld[sl + 2] = em.ap("scalar", lambda e, sl=sl, c=c, par=par: e.dma_start(
                    out=t_.ln_fm[par][:, c, :],
                    in_=t_.lnx_dram[512 * (sl + 2):512 * (sl + 3), 128 * c:128 * (c + 1)],
                    transpose=True), inc=("ad", 16), waits=[("t", tmm)])
        vprev = em.ap(V, lambda e, sl=sl: e.tensor_scalar(
            t_.probs_fm[:, 512 * sl:512 * (sl + 1)], ps, t_.bg_sb[:, 0:1], None, ALU.add),
            inc=("v", 1), waits=[("t", tmm)])
    em.ap(V, lambda e: e.reduce_max(t_.smx_m[:, :], t_.probs_fm[:, :], axis=AXL.X), dr=True)
    em.ap(V, lambda e: e.tensor_scalar_mul(t_.smx_m[:, :], t_.smx_m[:, :], -1.0), dr=True)
    sv = em.ap(S, lambda e: e.activation(
        t_.probs_fm[:, :], t_.probs_fm[:, :], ACT.Exp, bias=t_.smx_m[:, 0:1]),
        inc=("s", 1), waits=[("v", em.count("v"))], dr=True)
    em.ap(V, lambda e: e.reduce_sum(t_.smx_s[:, :], t_.probs_fm[:, :], axis=AXL.X),
          waits=[("s", sv)], dr=True)
    em.ap(V, lambda e: e.reciprocal(t_.smx_s[:, :], t_.smx_s[:, :]), dr=True)
    vp = em.ap(V, lambda e: e.tensor_scalar(
        t_.probs_fm[:, :], t_.probs_fm[:, :], t_.smx_s[:, 0:1], None, ALU.mult),
        inc=("v", 1), dr=True)
    return [("v", vp)]


def emit_ffn_experts(em, t_, io, pall, call_id, resid_pair):
    V, S, G, TE = "vector", "scalar", "gpsimd", "tensor"

    wsem = ["wd0", "wd1"]
    gsem = ["gd0", "gd1"]
    wdram = [io.w1_dram, io.w2_dram, io.w3_dram]

    em.ap(G, lambda e: e.load_library(libcfg.mlp))
    em.ap(G, lambda e: e.partition_broadcast(
        t_.bias_bc[:, :, :].rearrange("p l e -> p (l e)"), t_.bias_row[0:1, :], 128),
        inc=("g", 1), waits=[("cd", 1)])

    _regcache = {}

    def _getreg(eng):
        if "r" not in _regcache:
            _regcache["r"] = eng.alloc_register(f"nfreg{call_id}")
        return _regcache["r"]

    w_consumed = {0: None, 1: None}
    g_consumed = {0: None, 1: None}

    def emit_wload(li):
        e_, l = divmod(li, 3)
        par = li % 2
        waits = ([w_consumed[par]] if w_consumed[par]
                 else [("g", em.count("g")), ("od", 0), ("t", em.count("t"))])
        def b(eng, e_=e_, l=l, par=par):
            return eng.dma_start(
                out=t_.wbuf[par][:, :, :],
                in_=wdram[l][e_ * D:(e_ + 1) * D, :].rearrange("(kc p) o -> p kc o", p=128))
        return em.ap(G, b, inc=(wsem[par], 16), waits=waits)

    def emit_gather(e_):
        par = e_ % 2
        ql = e_ % 8
        def b(eng, e_=e_, par=par, ql=ql):
            r = _getreg(eng)
            eng.reg_load(r, t_.nf[16 * ql:16 * ql + 1, e_ // 8:e_ // 8 + 1])
            return eng.dma_gather(
                t_.tokbuf[par][:, :, :],
                t_.lnx_dram[:, :], t_.idx_all[:, 16 * e_:16 * (e_ + 1)],
                K, r, D, elem_step=D, transpose=True)
        waits = [("v", em.count("v")), ("ad", 0)]
        if g_consumed[par] is not None:
            waits.append(g_consumed[par])
        return em.ap(G, b, inc=(gsem[par], 16), waits=waits)

    wld = {}
    wld[0] = emit_wload(0)
    wld[1] = emit_wload(1)
    gath = {0: emit_gather(0)}

    for e_ in range(E):
        epar = e_ % 2
        if e_ + 1 < E:
            gath[e_ + 1] = emit_gather(e_ + 1)
        gready = [(gsem[epar], gath[e_])]
        li = 3 * e_
        par = li % 2
        wready = [(wsem[par], wld[li])]
        sevac = {0: em.count("s"), 1: em.count("s")}
        for oc in range(8):
            pp = t_.ps_ab[oc % 2]
            tmm = None
            for kc in range(8):
                tmm = em.ap(TE, lambda e, oc=oc, kc=kc, par=par, epar=epar, pp=pp: e.matmul(
                    pp, t_.wbuf[par][:, kc, 128 * oc:128 * (oc + 1)],
                    t_.tokbuf[epar][:, kc, :], start=(kc == 0), stop=(kc == 7)),
                    inc=("t", 1),
                    waits=(((wready + gready) if (oc == 0 and kc == 0) else [])
                           + ([("s", sevac[oc % 2])] if kc == 0 else [])))
            sevac[oc % 2] = em.ap(S, lambda e, oc=oc, e_=e_, pp=pp: e.activation(
                t_.h1[:, oc, :], pp, ACT.Relu,
                bias=t_.bias_bc[:, 0, e_:e_ + 1]),
                inc=("s", 1), waits=[("t", tmm)])
        w_consumed[par] = ("t", em.count("t"))
        wld[li + 2] = emit_wload(li + 2)
        s_h1 = em.count("s")
        li = 3 * e_ + 1
        par = li % 2
        wready = [(wsem[par], wld[li])]
        sevac = {0: s_h1, 1: s_h1}
        for oc in range(8):
            pp = t_.ps_ab[oc % 2]
            tmm = None
            for kc in range(8):
                tmm = em.ap(TE, lambda e, oc=oc, kc=kc, par=par, pp=pp: e.matmul(
                    pp, t_.wbuf[par][:, kc, 128 * oc:128 * (oc + 1)],
                    t_.h1[:, kc, :], start=(kc == 0), stop=(kc == 7)),
                    inc=("t", 1),
                    waits=((wready if (oc == 0 and kc == 0) else [])
                           + ([("s", sevac[oc % 2])] if kc == 0 else [])))
            sevac[oc % 2] = em.ap(S, lambda e, oc=oc, e_=e_, pp=pp: e.activation(
                t_.h2[:, oc, :], pp, ACT.Relu,
                bias=t_.bias_bc[:, 1, e_:e_ + 1]),
                inc=("s", 1), waits=[("t", tmm)])
        w_consumed[par] = ("t", em.count("t"))
        if li + 2 < 3 * E:
            wld[li + 2] = emit_wload(li + 2)
        s_h2 = em.count("s")
        li = 3 * e_ + 2
        par = li % 2
        wready = [(wsem[par], wld[li])]
        vevac = {0: em.count("v"), 1: em.count("v")}
        for sh in range(4):
            sl, half = sh // 2, sh % 2
            pp = t_.ps_cd[sh % 2]
            tmm = None
            for fc in range(8):
                tmm = em.ap(TE, lambda e, sl=sl, half=half, fc=fc, par=par, pp=pp: e.matmul(
                    pp, t_.h2[:, fc, 128 * sl:128 * (sl + 1)],
                    t_.wbuf[par][:, fc, 512 * half:512 * (half + 1)],
                    start=(fc == 0), stop=(fc == 7)),
                    inc=("t", 1),
                    waits=(((wready + [("s", s_h2)]) if sh == 0 and fc == 0 else [])
                           + ([("v", vevac[sh % 2])] if fc == 0 else [])))
            vevac[sh % 2] = em.ap(V, lambda e, sl=sl, half=half, e_=e_, epar=epar, pp=pp: e.tensor_scalar(
                t_.scat_src[epar][:, sl, 512 * half:512 * (half + 1)], pp,
                t_.bias_bc[:, 2, e_:e_ + 1], t_.vals_tm[:, e_, sl:sl + 1],
                ALU.add, op1=ALU.mult),
                inc=("v", 1), waits=[("t", tmm), ("sct", 0)])
        v_l3 = em.count("v")
        w_consumed[par] = ("t", em.count("t"))
        if li + 2 < 3 * E:
            wld[li + 2] = emit_wload(li + 2)
        g_consumed[epar] = ("v", v_l3)
        def bscat(eng, e_=e_, epar=epar):
            ql = e_ % 8
            r = _getreg(eng)
            eng.reg_load(r, t_.nf[16 * ql:16 * ql + 1, e_ // 8:e_ // 8 + 1])
            return eng.dma_scatter_add(
                resid_pair[0][:, :, :], t_.scat_src[epar][:, :, :],
                t_.idx_all[:, 16 * e_:16 * (e_ + 1)], K, r, D,
                sbuf_tokens_per_rank=128, parity_reg=0,
                out_ap_other=resid_pair[1][:, :, :])
        em.ap(G, bscat, inc=("sct", 16), waits=[("v", v_l3), ("sct", 0)])
    return ("sct", em.count("sct"))


def alloc_io_ffn(nc):
    io = types.SimpleNamespace()
    io.x_dram = nc.dram_tensor("x_in", [T, D], F32, kind="ExternalInput").ap()
    io.wg_dram = nc.dram_tensor("ffn_Wg", [D, E], F32, kind="ExternalInput").ap()
    io.bg_dram = nc.dram_tensor("ffn_bg", [E, 1], F32, kind="ExternalInput").ap()
    io.w1_dram = nc.dram_tensor("ffn_w1", [E * D, D], F32, kind="ExternalInput").ap()
    io.w2_dram = nc.dram_tensor("ffn_w2", [E * D, D], F32, kind="ExternalInput").ap()
    io.w3_dram = nc.dram_tensor("ffn_w3", [E * D, D], F32, kind="ExternalInput").ap()
    io.b1_dram = nc.dram_tensor("ffn_b1", [E, 1], F32, kind="ExternalInput").ap()
    io.b2_dram = nc.dram_tensor("ffn_b2", [E, 1], F32, kind="ExternalInput").ap()
    io.b3_dram = nc.dram_tensor("ffn_b3", [E, 1], F32, kind="ExternalInput").ap()
    return io


def emit_ffn_bias_loads(em, t_, io):
    for l, dram in enumerate([io.b1_dram, io.b2_dram, io.b3_dram]):
        em.ap("sync", lambda e, l=l, dram=dram: e.dma_start(
            out=t_.bias_row[0:1, 16 * l:16 * (l + 1)],
            in_=dram.rearrange("e one -> one e")), inc=("cd", 16))


def emit_gating_weight_loads(em, t_, io):
    em.ap("gpsimd", lambda e: e.dma_start(
        out=t_.wg_sb[:, :, :],
        in_=io.wg_dram.rearrange("(kc p) e -> p kc e", p=128)), inc=("cw", 16))
    em.ap("sync", lambda e: e.dma_start(
        out=t_.bg_sb[:, :], in_=io.bg_dram), inc=("cd", 16))


def emit_resid_load(em, t_, io, resid_pair):
    for par in range(2):
        em.ap("sync", lambda e, par=par: e.dma_start(
            out=resid_pair[par][:, :, :],
            in_=io.x_dram.rearrange("(g two p) d -> two p g d", two=2, p=128)[par:par + 1].rearrange(
                "one p g d -> (one p) g d")),
            inc=("cd", 16))


def emit_resid_store(em, t_, out_dram, resid_pair, waits):
    last = None
    for par in range(2):
        last = em.ap("sync", lambda e, par=par: e.dma_start(
            out=out_dram.rearrange("(g two p) d -> two p g d", two=2, p=128)[par:par + 1].rearrange(
                "one p g d -> (one p) g d"),
            in_=resid_pair[par][:, :, :]),
            inc=("od", 16), waits=waits)
    return ("od", last)


def emit_ffn_call(em, t_, io, pall, resid_pair, call_id, ready=()):
    lnx_ready = emit_ln(em, t_, resid_pair, "ad", ready=ready)
    probs_ready = emit_gating(em, t_, pall, lnx_ready)
    emit_selection(em, t_, t_.probs_fm[:, :], probs_ready)
    return emit_ffn_experts(em, t_, io, pall, call_id, resid_pair)



def alloc_io_attn(nc):
    io = types.SimpleNamespace()
    io.wq = nc.dram_tensor("attn_Wq", [D, D], F32, kind="ExternalInput").ap()
    io.wk = nc.dram_tensor("attn_Wk", [D, D], F32, kind="ExternalInput").ap()
    io.wv = nc.dram_tensor("attn_Wv", [D, D], F32, kind="ExternalInput").ap()
    io.wout = nc.dram_tensor("attn_Wout", [D, D], F32, kind="ExternalInput").ap()
    io.wbeta = nc.dram_tensor("attn_Wbeta", [16, D], F32, kind="ExternalInput").ap()
    io.wmix = nc.dram_tensor("attn_Wmix", [16, D], F32, kind="ExternalInput").ap()
    io.wgate = nc.dram_tensor("attn_Wgate", [8, D], F32, kind="ExternalInput").ap()
    io.bbeta = nc.dram_tensor("attn_bbeta", [16, 1], F32, kind="ExternalInput").ap()
    io.bmix = nc.dram_tensor("attn_bmix", [16, 1], F32, kind="ExternalInput").ap()
    io.bgate = nc.dram_tensor("attn_bgate", [8, 1], F32, kind="ExternalInput").ap()
    io.bout = nc.dram_tensor("attn_bout", [1, D], F32, kind="ExternalInput").ap()
    io.ln_g = nc.dram_tensor("attn_ln_g", [1, D], F32, kind="ExternalInput").ap()
    io.ln_b = nc.dram_tensor("attn_ln_b", [1, D], F32, kind="ExternalInput").ap()
    io.wq_bf = nc.dram_tensor("wq_bf", [D, D], BF16, kind="Internal").ap()
    io.wk_bf = nc.dram_tensor("wk_bf", [D, D], BF16, kind="Internal").ap()
    io.wv_bf = nc.dram_tensor("wv_bf", [D, D], BF16, kind="Internal").ap()
    io.wout_bf = nc.dram_tensor("wout_bf", [D, D], BF16, kind="Internal").ap()
    io.ydram = nc.dram_tensor("ydram", [T, D], BF16, kind="Internal").ap()
    return io


def alloc_attn(nc, stack, cs, t_):
    sb = make_sb(nc, stack)
    w0flat = t_.wbuf[0].rearrange("p a b -> p (a b)")
    w0f32 = w0flat.bitcast(F32)
    t_.pool_d = w0f32[:, 0:2048]
    t_.cb_tm = w0f32[:, 2048:3456].rearrange("p (r c) -> p r c", r=88)
    t_.wbmgT = w0flat[:, 6912:7232].rearrange("p (kc f) -> p kc f", kc=8)
    t_.wbmg_nat = w0f32[0:40, 1024:2048]
    t_.pool_e = sb("pool_e", [128, 1024], F32)
    t_.scat_f = t_.pool_e.rearrange("p (a b) -> p a b", a=4)
    t_.bbmg = sb("bbmg", [40, 1], F32)
    t_.g_rep = sb("g_rep", [128, D], F32)
    t_.b_rep = sb("b_rep", [128, D], BF16)
    t_.grow = t_.pool_e[0:1, :]
    t_.brow = t_.pool_e[32:33, :]

    t_.zall = sb("zall", [128, 16], F32)
    t_.ln2_mv = sb("ln2_mv", [128, NB, 2], F32)
    t_.perm32 = t_.ln_bf.rearrange("p a b -> p (a b)").bitcast(F32)[0:32, :]
    return t_


def attn_views(t_):
    v = types.SimpleNamespace()
    v.wqT = t_.tokbuf[0]
    v.wkT = t_.tokbuf[1]
    v.wvT = t_.h1
    v.woutT = t_.wbuf[0]
    qk = t_.wbuf[1].rearrange("p a b -> p (a b)")
    v.q_fm = qk[:, 0:4096].rearrange("p (hh t) -> p hh t", hh=2)
    v.k_fm = qk[:, 4096:8192].rearrange("p (hh t) -> p hh t", hh=2)
    v.v_tm = t_.scat_src[0].rearrange("p a b -> p (a b)").bitcast(BF16).rearrange(
        "p (c f) -> p c f", c=16)
    v.S01 = t_.scat_src[1].rearrange("p a b -> p (a b)").rearrange(
        "p (hj d) -> p hj d", hj=16)
    v.u = t_.ln_bf.rearrange("p a b -> p (a b)").bitcast(F32).rearrange(
        "p (s x) -> p s x", s=2)
    v.L = t_.flags_fm
    v.expL = t_.pool_d[64:80, :]
    v.ecd = t_.probs_fm
    v.yfm = [lf[:, :, 0:256] for lf in t_.ln_fm]
    return v


def emit_attn_w_staging(em, t_, io):
    G = "gpsimd"
    prev_store = None
    for wi, (wdr, wbf) in enumerate([(io.wq, io.wq_bf), (io.wk, io.wk_bf),
                                     (io.wv, io.wv_bf), (io.wout, io.wout_bf)]):
        w = [("od", prev_store)] if prev_store is not None else []
        ldw = em.ap(G, lambda e, wdr=wdr: e.dma_start(
            out=t_.wbuf[0][:, :, :],
            in_=wdr.rearrange("(kc p) o -> p kc o", p=128)),
            inc=("kd", 16), waits=w)
        prev_store = em.ap("sync", lambda e, wbf=wbf: e.dma_start(
            out=wbf.rearrange("(kc p) o -> p kc o", p=128),
            in_=t_.wbuf[0][:, :, :]),
            inc=("od", 16), waits=[("kd", ldw)])
    for i, (bdr, r0, r1) in enumerate([(io.bbeta, 0, 16), (io.bmix, 16, 32), (io.bgate, 32, 40)]):
        em.ap("sync", lambda e, bdr=bdr, r0=r0, r1=r1: e.dma_start(
            out=t_.bbmg[r0:r1, :], in_=bdr), inc=("od", 16))
    return ("od", em.count("od")), ("kd", em.count("kd"))


def emit_attn(em, t_, io, av, resid_pair, ffn_done, stop_after_y=False,
              passes=range(4), chunks=range(NCH), dbg_hook=None, wout_hook=None,
              scan_hook=None):
    V, S, G, TE = "vector", "scalar", "gpsimd", "tensor"
    nc = em.nc
    wk2 = t_.h2.rearrange("p a b -> p (a b)")
    ktm_bf = wk2[:, 0:256].rearrange("p (j x) -> p j x", j=2)
    sbf = wk2[:, 256:512].rearrange("p (j x) -> p j x", j=2)
    zc = wk2[:, 512:514]
    vg = wk2[:, 514:644]
    vhat = wk2[:, 644:908].rearrange("p (j x) -> p j x", j=2)
    vga = wk2[:, 908:1172].rearrange("p (j x) -> p j x", j=2)
    m_sc = wk2[:, 1172:1300]
    ytmp = wk2[:, 1300:1556]
    scA = t_.pool_e[:, 0:128]
    scB = t_.pool_e[:, 128:256]
    kv = t_.pool_e[:, 256:272]

    lnx_ready = emit_ln(em, t_, resid_pair, "ad", ready=[ffn_done])

    grow_ld = em.ap("sync", lambda e: e.dma_start(out=t_.grow[0:1, :], in_=io.ln_g),
                    inc=("ad", 16),
                    waits=[("v", em.count("v")), ("g", em.count("g")), ffn_done])
    em.ap(G, lambda e: e.load_library(libcfg.mlp), waits=[ffn_done])
    gbc = em.ap(G, lambda e: e.partition_broadcast(t_.g_rep[:, :], t_.grow[0:1, :], 128),
                inc=("g", 1), waits=[("od", 0), ("ad", grow_ld)])

    wbl = None
    for i, (wdr, r0, r1) in enumerate([(io.wbeta, 0, 16), (io.wmix, 16, 32), (io.wgate, 32, 40)]):
        wbl = em.ap(G, lambda e, wdr=wdr, r0=r0, r1=r1: e.dma_start(
            out=t_.wbmg_nat[r0:r1, :], in_=wdr),
            inc=("kd", 16), waits=[("t", em.count("t")), ("s", em.count("s"))])
    for c in range(8):
        tt = em.ap(TE, lambda e, c=c: e.transpose(
            t_.ps_kt[0][:, 0:40], t_.wbmg_nat[0:40, 128 * c:128 * (c + 1)],
            t_.c_id128f[0:40, 0:40]),
            inc=("t", 1), waits=[("kd", 0), ("s", em.count("s"))])
        em.ap(S, lambda e, c=c: e.activation(
            t_.wbmgT[:, c, :], t_.ps_kt[0][:, 0:40], ACT.Identity),
            inc=("s", 1), waits=[("t", tt)])
    s_wbmgT = em.count("s")

    vpd = em.count("v")
    lastmm = {0: None, 1: None}
    for sl in range(4):
        par = sl % 2
        ld = None
        for c in range(8):
            w = [lnx_ready] + ([("t", lastmm[par])] if lastmm[par] else [])
            ld = em.ap("scalar", lambda e, sl=sl, c=c, par=par: e.dma_start(
                out=t_.ln_fm[par][:, c, :],
                in_=t_.lnx_dram[512 * sl:512 * (sl + 1), 128 * c:128 * (c + 1)],
                transpose=True), inc=("ad", 16), waits=w)
        tmm = None
        for kc in range(8):
            tmm = em.ap(TE, lambda e, sl=sl, kc=kc, par=par: e.matmul(
                t_.ps_cd[1][0:40, 0:512], t_.wbmgT[:, kc, :], t_.ln_fm[par][:, kc, :],
                start=(kc == 0), stop=(kc == 7)),
                inc=("t", 1),
                waits=([("ad", ld), ("s", s_wbmgT), ("v", vpd)] if kc == 0 else []))
        lastmm[par] = tmm
        vpd = em.ap(V, lambda e, sl=sl: e.tensor_scalar(
            t_.pool_d[0:40, 512 * sl:512 * (sl + 1)], t_.ps_cd[1][0:40, 0:512],
            t_.bbmg[:, 0:1], None, ALU.add),
            inc=("v", 1), waits=[("t", tmm)])

    bet = t_.pool_d[0:16, :]
    mx = t_.pool_d
    sb1 = em.ap(S, lambda e: e.activation(bet, bet, ACT.Sigmoid),
                inc=("s", 1), waits=[("v", vpd)], dr=True)
    em.ap(V, lambda e: e.tensor_scalar(bet, bet, t_.c_r_hj[:, 0:1], None, ALU.mult),
          waits=[("s", sb1)], dr=True)
    em.ap(V, lambda e: e.tensor_single_scalar(bet, bet, 0.9999, ALU.min), dr=True)
    vbe = em.ap(V, lambda e: e.tensor_single_scalar(bet, bet, 1e-20, ALU.max),
                inc=("v", 1), dr=True)
    sbl = em.ap(S, lambda e: e.activation(bet, bet, ACT.Ln),
                inc=("s", 1), waits=[("v", vbe)], dr=True)
    vL = None
    for c in range(NCH):
        vL = em.ap(V, lambda e, c=c: e.tensor_tensor_scan(
            av.L[:, 128 * c:128 * (c + 1)], bet[:, 128 * c:128 * (c + 1)],
            t_.zeros_scan[0:16, 0, :], 0.0, ALU.add, ALU.add),
            inc=("v", 1), waits=([("s", sbl)] if c == 0 else []), dr=(c == 0))
    em.ap(V, lambda e: e.tensor_copy(
        av.ecd[:, :].rearrange("p (c f) -> p c f", f=128),
        av.L[:, 127::128].unsqueeze(2).broadcast_to([16, 16, 128])),
        waits=[("v", vL)], dr=True)
    vec1 = em.ap(V, lambda e: e.tensor_sub(av.ecd[:, :], av.ecd[:, :], av.L[:, :]),
                 inc=("v", 1), dr=True)
    sEc = em.ap(S, lambda e: e.activation(av.ecd[:, :], av.ecd[:, :], ACT.Exp),
                inc=("s", 1), waits=[("v", vec1)], dr=True)
    vla = em.ap(V, lambda e: e.tensor_copy(
        t_.pool_d[0:16, :].rearrange("p (c f) -> p c f", f=128),
        av.L[:, 127::128].unsqueeze(2).broadcast_to([16, 16, 128])),
        inc=("v", 1), waits=[("v", vL)], dr=True)
    vla = em.ap(S, lambda e: e.activation(
        t_.pool_d[0:16, :], t_.pool_d[0:16, :], ACT.Exp),
        inc=("s", 1), waits=[("v", vla)])
    ssg = em.ap(S, lambda e: e.activation(mx[32:40, :], mx[32:40, :], ACT.Sigmoid),
                inc=("s", 1), waits=[("v", vpd)], dr=True)

    perm = t_.perm32
    dcb = nc.dram_tensor("attn_dcb", [88, 128, 16], F32, kind="Internal").ap()
    dwr = None
    emitted = {}

    def bounce(srcap, r0, r1, skip, w):
        nonlocal dwr
        n = r1 - r0
        ncp = srcap.shape[0]
        vp = em.ap(V, lambda e, srcap=srcap, ncp=ncp: e.tensor_copy(
            perm[0:ncp, :].rearrange("r (p c) -> r c p", c=16),
            srcap.rearrange("r (c p) -> r c p", p=128)),
            inc=("v", 1), waits=(w + ([("ad", dwr)] if dwr else [])), dr=True)
        dwr = em.ap("sync", lambda e, r0=r0, n=n, skip=skip: e.dma_start(
            out=dcb[r0:r0 + n, :, :],
            in_=perm[skip:skip + n, :].rearrange("r (p c) -> r p c", c=16)),
            inc=("ad", 16), waits=[("v", vp)])
        return dwr

    bounce(av.ecd[:, :], 16, 32, 0, [("s", sEc)])
    bounce(mx[0:32, :], 32, 48, 16, [("s", ssg), ("s", vla)])
    bounce(mx[32:40, :], 48, 56, 0, [("s", ssg)])
    dwr_ebc = bounce(t_.pool_d[0:16, :], 56, 72, 0, [("s", vla)])
    vcc = em.ap(V, lambda e: e.tensor_copy(
        t_.pool_d[0:16, :].rearrange("p (c f) -> p c f", f=128),
        av.L[:, 127::128].unsqueeze(2).broadcast_to([16, 16, 128])),
        inc=("v", 1), waits=[("ad", dwr_ebc)], dr=True)
    scc = em.ap(S, lambda e: e.activation(
        t_.pool_d[0:16, :], t_.pool_d[0:16, :], ACT.Exp,
        scale=t_.c_scal[0:16, 2:3]),
        inc=("s", 1), waits=[("v", vcc)])
    dwr_cbc = bounce(t_.pool_d[0:16, :], 72, 88, 0, [("s", scc)])
    vca = em.ap(V, lambda e: e.scalar_tensor_tensor(
        t_.pool_d[0:16, :].rearrange("p (c f) -> p c f", f=128),
        av.L[:, 127::128].unsqueeze(2).broadcast_to([16, 16, 128]),
        t_.c_scal[0:16, 2:3],
        av.L[:, :].rearrange("p (c f) -> p c f", f=128),
        ALU.mult, ALU.subtract),
        inc=("v", 1), waits=[("ad", dwr_cbc)], dr=True)
    sca = em.ap(S, lambda e: e.activation(
        t_.pool_d[0:16, :], t_.pool_d[0:16, :], ACT.Exp),
        inc=("s", 1), waits=[("v", vca)])
    bounce(t_.pool_d[0:16, :], 0, 16, 0, [("s", sca)])
    dcbr = em.ap("sync", lambda e: e.dma_start(
        out=t_.cb_tm[:, :, :], in_=dcb.rearrange("r p c -> p r c")),
        inc=("ad", 16), waits=[("ad", dwr)])
    cbv = t_.cb_tm[:, 32:48, :].rearrange("p (h j) c -> p h j c", j=2)
    vw = em.ap(V, lambda e: e.tensor_sub(
        cbv[:, :, 0, :], cbv[:, :, 0, :], cbv[:, :, 1, :]),
        inc=("v", 1), waits=[("ad", dcbr)], dr=True)
    sw = em.ap(S, lambda e: e.activation(
        cbv[:, :, 0, :], cbv[:, :, 0, :], ACT.Sigmoid),
        inc=("s", 1), waits=[("v", vw)])
    vw = em.ap(V, lambda e: e.tensor_scalar(
        cbv[:, :, 1, :], cbv[:, :, 0, :], -1.0, 1.0, ALU.mult, op1=ALU.add),
        inc=("v", 1), waits=[("s", sw)])
    cbw = [("ad", dcbr), ("v", vw), ("s", sw), ("g", gbc)]

    def cb(row):
        return t_.cb_tm[:, row, :]

    em.ap(V, lambda e: e.memset(av.S01[:, :, :], 0.0), waits=[("sct", 0)])
    vS = em.ap(V, lambda e: e.memset(t_.zall[:, :], 0.0), inc=("v", 1))

    em.wait("scalar", "ad", dcbr)
    em.wait("scalar", "g", gbc)
    em.wait("scalar", "v", vw)
    em.wait("vector", "ad", dcbr)
    em.wait("vector", "g", gbc)
    em.wait("vector", "s", sw)

    t_consumed_w = em.count("t")
    v_mp = {0: em.count("v"), 1: em.count("v")}
    v_msc = em.count("v")
    v_yprev = em.count("v")
    t_Aprev = em.count("t")
    t_Bzprev = em.count("t")
    t_Sprev = em.count("t")
    v_state = vS
    psA_w = []
    psB_w = []
    psS_w = []
    for hp in passes:
        h0 = 2 * hp
        wl = None
        for kc in range(8):
            for (dst, wbf) in [(av.wqT, io.wq_bf), (av.wkT, io.wk_bf), (av.wvT, io.wv_bf)]:
                wl = em.ap("scalar", lambda e, kc=kc, dst=dst, wbf=wbf, hp=hp: e.dma_start(
                    out=dst[:, kc, :],
                    in_=wbf[256 * hp:256 * (hp + 1), 128 * kc:128 * (kc + 1)],
                    transpose=True),
                    inc=("dd0", 16), waits=[("od", 0), ("t", t_consumed_w)])
        wT_ready = ("dd0", wl)
        v_prev = em.count("v")
        s_prev = em.count("s")
        for sl in range(4):
            par = sl % 2
            ld = None
            for c in range(8):
                ld = em.ap("scalar", lambda e, sl=sl, c=c, par=par: e.dma_start(
                    out=t_.ln_fm[par][:, c, :],
                    in_=t_.lnx_dram[512 * sl:512 * (sl + 1), 128 * c:128 * (c + 1)],
                    transpose=True), inc=("ad", 16),
                    waits=[lnx_ready, ("t", em.count("t"))])
            for hh in range(2):
                for (wt, dstfm) in [(av.wqT, av.q_fm), (av.wkT, av.k_fm)]:
                    tmm = None
                    for kc in range(8):
                        tmm = em.ap(TE, lambda e, wt=wt, kc=kc, par=par, hh=hh: e.matmul(
                            t_.ps_cd[1][:, 0:512], wt[:, kc, 128 * hh:128 * (hh + 1)],
                            t_.ln_fm[par][:, kc, :],
                            start=(kc == 0), stop=(kc == 7)),
                            inc=("t", 1),
                            waits=([("ad", ld), wT_ready, ("v", v_prev), ("s", s_prev)]
                                   if kc == 0 else []))
                    vm = em.ap(V, lambda e: e.tensor_scalar_min(
                        t_.pool_e[:, 0:512], t_.ps_cd[1][:, 0:512], 0.0),
                        inc=("v", 1), waits=[("t", tmm), ("g", gbc)])
                    s_prev = em.ap(S, lambda e: e.activation(
                        t_.pool_e[:, 512:1024], t_.pool_e[:, 0:512], ACT.Exp),
                        inc=("s", 1), waits=[("v", vm)])
                    v_prev = em.ap(V, lambda e, dstfm=dstfm, sl=sl, hh=hh: e.scalar_tensor_tensor(
                        dstfm[:, hh, 512 * sl:512 * (sl + 1)],
                        t_.ps_cd[1][:, 0:512], 0.0, t_.pool_e[:, 512:1024], ALU.max, ALU.add),
                        inc=("v", 1), waits=[("s", s_prev)])
            for cc in range(4):
                tc = 4 * sl + cc
                pp = t_.ps_ab[cc % 2]
                tmm = None
                for kc in range(8):
                    tmm = em.ap(TE, lambda e, cc=cc, kc=kc, par=par, pp=pp: e.matmul(
                        pp[:, :], t_.ln_fm[par][:, kc, 128 * cc:128 * (cc + 1)],
                        av.wvT[:, kc, :], start=(kc == 0), stop=(kc == 7)),
                        inc=("t", 1),
                        waits=([("s", em.count("s"))] if kc == 0 else []))
                em.ap(S, lambda e, tc=tc, pp=pp: e.activation(
                    av.v_tm[:, tc, :], pp[:, :], ACT.Identity),
                    inc=("s", 1), waits=[("t", tmm)])
        s_v = em.count("s")
        v_q = v_prev

        psB_w = psB_w + [("v", v_q)]
        for c in chunks:
            csl = slice(128 * c, 128 * (c + 1))
            for hh in range(2):
                h = h0 + hh
                tsc = em.ap(TE, lambda e, hh=hh, csl=csl: e.matmul(
                    t_.ps_sc[:, :], av.k_fm[:, hh, csl], av.q_fm[:, hh, csl],
                    start=True, stop=True),
                    inc=("t", 1), waits=[("v", v_q), ("v", v_msc)])
                tkt = em.ap(TE, lambda e, hh=hh, csl=csl: e.transpose(
                    t_.ps_ktb[hh][:, :], av.k_fm[:, hh, csl], t_.c_id128[:, :]),
                    inc=("t", 1), waits=[("s", em.count("s"))])
                skt = em.ap(S, lambda e, hh=hh: e.activation(
                    ktm_bf[:, hh, :], t_.ps_ktb[hh][:, :], ACT.Identity),
                    inc=("s", 1), waits=[("t", tkt), ("t", t_Sprev)])
                v_msc = em.ap(V, lambda e: e.tensor_mul(
                    m_sc[:, :], t_.ps_sc[:, :], t_.c_caus[:, :]),
                    inc=("v", 1), waits=[("t", tsc), ("t", t_Aprev)])
                svg = em.ap(S, lambda e, h=h, c=c: e.activation(
                    vg[:, 0:128], av.v_tm[:, c, 128 * (h % 2):128 * (h % 2) + 128],
                    ACT.Identity, scale=cb(48 + h)[:, c:c + 1]),
                    inc=("s", 1), waits=[("s", s_v)])
                svg = em.ap(S, lambda e, h=h, c=c: e.activation(
                    vg[:, 128:129], cb(48 + h)[:, c:c + 1], ACT.Identity),
                    inc=("s", 1))
                vv1 = em.ap(V, lambda e, h=h, c=c: e.tensor_tensor(
                    vga[:, :, 0:129],
                    vg[:, 0:129].unsqueeze(1).broadcast_to([128, 2, 129]),
                    t_.cb_tm[:, 2 * h:2 * h + 2, c:c + 1].broadcast_to([128, 2, 129]),
                    ALU.mult),
                    inc=("v", 1), waits=[("s", svg), ("t", t_Aprev)])
                vv2 = em.ap(V, lambda e, h=h, c=c: e.tensor_tensor(
                    vhat[:, :, 0:129],
                    vg[:, 0:129].unsqueeze(1).broadcast_to([128, 2, 129]),
                    t_.cb_tm[:, 16 + 2 * h:18 + 2 * h, c:c + 1].broadcast_to([128, 2, 129]),
                    ALU.mult),
                    inc=("v", 1), waits=[("t", t_Sprev)])
                szc = em.ap(S, lambda e, h=h: e.activation(
                    zc[:, :], t_.zall[:, 2 * h:2 * h + 2], ACT.Identity),
                    inc=("s", 1), waits=[("v", v_state), ("t", t_Bzprev)])
                ssb = em.ap(S, lambda e, h=h: e.activation(
                    sbf[:, :, :].rearrange("p j x -> p (j x)"),
                    av.S01[:, 2 * h:2 * h + 2, :].rearrange("p hj x -> p (hj x)"),
                    ACT.Identity),
                    inc=("s", 1))
                tA = em.ap(TE, lambda e: e.matmul(
                    t_.ps_A[:, :, :].rearrange("p j x -> p (j x)"),
                    m_sc[:, :], vga[:, :, 0:129],
                    start=True, stop=True),
                    inc=("t", 1), waits=[("v", vv1)] + psA_w)
                t_Aprev = tA
                tBz = em.ap(TE, lambda e, hh=hh, csl=csl: e.matmul(
                    t_.ps_Bz[:, :], av.q_fm[:, hh, csl], wk2[:, 256:514],
                    start=True, stop=True),
                    inc=("t", 1), waits=[("s", ssb)] + psB_w)
                t_Bzprev = tBz
                tS = em.ap(TE, lambda e, hh=hh: e.matmul(
                    t_.ps_S[:, :, :].rearrange("p j x -> p (j x)"),
                    ktm_bf[:, hh, :], vhat[:, :, 0:129],
                    start=True, stop=True),
                    inc=("t", 1), waits=[("s", skt), ("v", vv2)] + psS_w)
                t_Sprev = tS
                vd = em.ap(V, lambda e, h=h, c=c: e.tensor_tensor(
                    kv[:, 0:2],
                    t_.cb_tm[:, 72 + 2 * h:74 + 2 * h, c:c + 1].rearrange("p r o -> p (r o)"),
                    t_.ps_Bz[:, 256:258], ALU.mult),
                    inc=("v", 1), waits=[("t", tBz), ("v", v_yprev)])
                vd = em.ap(V, lambda e, h=h, c=c: e.tensor_scalar(
                    kv[:, 2:4],
                    t_.cb_tm[:, 2 * h:2 * h + 2, c:c + 1].rearrange("p r o -> p (r o)"),
                    t_.c_scal[:, 1:2], None, ALU.mult),
                    inc=("v", 1))
                vst = em.ap(V, lambda e, h=h, c=c: e.tensor_tensor(
                    av.S01[:, 2 * h:2 * h + 2, :], av.S01[:, 2 * h:2 * h + 2, :],
                    t_.cb_tm[:, 56 + 2 * h:58 + 2 * h, c:c + 1].broadcast_to([128, 2, 128]),
                    ALU.mult),
                    inc=("v", 1), waits=[("s", ssb)])
                vd = em.ap(V, lambda e: e.tensor_add(
                    kv[:, 2:4], kv[:, 2:4],
                    t_.ps_A[:, :, 128:129].rearrange("p j o -> p (j o)")),
                    inc=("v", 1), waits=[("t", tA)])
                em.ap(V, lambda e, h=h, c=c: e.tensor_tensor(
                    t_.zall[:, 2 * h:2 * h + 2], t_.zall[:, 2 * h:2 * h + 2],
                    t_.cb_tm[:, 56 + 2 * h:58 + 2 * h, c:c + 1].rearrange("p r o -> p (r o)"),
                    ALU.mult),
                    waits=[("s", szc)])
                vd = em.ap(V, lambda e: e.tensor_add(kv[:, 10:12], kv[:, 0:2], kv[:, 2:4]),
                           inc=("v", 1))
                vsa = em.ap(V, lambda e, h=h: e.tensor_add(
                    av.S01[:, 2 * h:2 * h + 2, :], av.S01[:, 2 * h:2 * h + 2, :],
                    t_.ps_S[:, :, 0:128]),
                    inc=("v", 1), waits=[("t", tS)])
                vd = em.ap(V, lambda e: e.reciprocal(kv[:, 8:10], kv[:, 10:12]),
                           inc=("v", 1))
                v_state = em.ap(V, lambda e, h=h: e.tensor_add(
                    t_.zall[:, 2 * h:2 * h + 2], t_.zall[:, 2 * h:2 * h + 2],
                    t_.ps_S[:, :, 128:129].rearrange("p j o -> p (j o)")),
                    inc=("v", 1))
                vd = em.ap(V, lambda e, h=h, c=c: e.tensor_tensor(
                    kv[:, 4:6], kv[:, 8:10],
                    t_.cb_tm[:, 32 + 2 * h:34 + 2 * h, c:c + 1].rearrange("p r o -> p (r o)"),
                    ALU.mult),
                    inc=("v", 1))
                vd = em.ap(V, lambda e, h=h, c=c: e.tensor_tensor(
                    kv[:, 6:8], kv[:, 4:6],
                    t_.cb_tm[:, 72 + 2 * h:74 + 2 * h, c:c + 1].rearrange("p r o -> p (r o)"),
                    ALU.mult),
                    inc=("v", 1), dr=True)
                sy = em.ap(S, lambda e: e.activation(
                    scA, t_.ps_A[:, 0, 0:128], ACT.Identity, scale=kv[:, 4:5]),
                    inc=("s", 1), waits=[("v", vd)])
                vy = em.ap(V, lambda e: e.scalar_tensor_tensor(
                    scB, t_.ps_Bz[:, 0:128], kv[:, 6:7], scA, ALU.mult, ALU.add),
                    inc=("v", 1), waits=[("s", sy)])
                vy = em.ap(V, lambda e: e.scalar_tensor_tensor(
                    scA, t_.ps_A[:, 1, 0:128], kv[:, 5:6], scB, ALU.mult, ALU.add),
                    inc=("v", 1), dr=True)
                vy = em.ap(V, lambda e, hh=hh: e.scalar_tensor_tensor(
                    ytmp[:, 128 * hh:128 * (hh + 1)], t_.ps_Bz[:, 128:256], kv[:, 7:8],
                    scA, ALU.mult, ALU.add),
                    inc=("v", 1), waits=[("od", 0)], dr=True)
                v_mp[hh] = vy
                v_yprev = vy
                psA_w = [("v", vy), ("s", sy)]
                psB_w = [("v", vy), ("s", sy)]
                psS_w = [("v", v_state)]
                if scan_hook and scan_hook(em, hp, c, hh, v_state):
                    return ("od", em.count("od"))
            y_wr = em.ap("sync", lambda e, c=c, hp=hp: e.dma_start(
                out=io.ydram[128 * c:128 * (c + 1), 256 * hp:256 * (hp + 1)],
                in_=ytmp[:, :]),
                inc=("od", 16), waits=[("v", v_mp[0]), ("v", v_mp[1])])
            if dbg_hook:
                dbg_hook(em, hp, c, y_wr)
        t_consumed_w = em.count("t")

    if stop_after_y:
        return ("od", em.count("od"))

    vfin = em.count("v")
    sfin = em.count("s")
    tfin = em.count("t")
    bout_bf = wk2[0:1, 0:1024]
    bout_ld = em.ap(G, lambda e: e.dma_start(out=bout_bf, in_=io.bout),
                    inc=("kd", 16), waits=[("v", vfin), ("s", sfin), ("t", tfin)])
    brow_bf = wk2[0:1, 1024:2048]
    brow_ld = em.ap(G, lambda e: e.dma_start(out=brow_bf, in_=io.ln_b),
                    inc=("kd", 16), waits=[("od", 0)])
    tb = None
    for half in range(2):
        tb = em.ap(TE, lambda e, half=half: e.matmul(
            t_.pb[6 + half][:, 0:512], t_.c_ones_row[0:1, :],
            brow_bf[0:1, 512 * half:512 * (half + 1)],
            start=True, stop=True), inc=("t", 1),
            waits=([("kd", brow_ld), ("v", vfin), ("s", sfin)] if half == 0 else []))
    sbrep = None
    for half in range(2):
        sbrep = em.ap(S, lambda e, half=half: e.activation(
            t_.b_rep[:, 512 * half:512 * (half + 1)], t_.pb[6 + half][:, 0:512],
            ACT.Identity),
            inc=("s", 1), waits=([("t", tb)] if half == 0 else []))
    wo = None
    for kc in range(8):
        wo = em.ap("scalar", lambda e, kc=kc: e.dma_start(
            out=av.woutT[:, kc, :], in_=io.wout_bf[:, 128 * kc:128 * (kc + 1)],
            transpose=True), inc=("dd0", 16),
            waits=[("t", tfin), ("od", 0), ("v", vfin), ("s", sfin)])
    for sl in range(8):
        par = sl % 2
        yld = None
        for fcc in range(8):
            yld = em.ap("scalar", lambda e, sl=sl, fcc=fcc, par=par: e.dma_start(
                out=av.yfm[par][:, fcc, :],
                in_=io.ydram[256 * sl:256 * (sl + 1), 128 * fcc:128 * (fcc + 1)],
                transpose=True), inc=("ad", 16),
                waits=[("od", 0), ("t", em.count("t"))])
        for cc in range(2):
            tc = 2 * sl + cc
            rt = resid_pair[tc % 2]
            g = tc // 2
            for half in range(2):
                pp = t_.ps_cd[half]
                tmm = em.ap(TE, lambda e, half=half, pp=pp: e.matmul(
                    pp, t_.c_ones_row[0:1, :], bout_bf[:, 512 * half:512 * (half + 1)],
                    start=True, stop=False),
                    inc=("t", 1),
                    waits=[("ad", yld), ("dd0", wo), ("kd", bout_ld),
                           ("v", vfin), ("s", sfin)])
                for fc in range(8):
                    tmm = em.ap(TE, lambda e, half=half, fc=fc, cc=cc, par=par, pp=pp: e.matmul(
                        pp, av.yfm[par][:, fc, 128 * cc:128 * (cc + 1)],
                        av.woutT[:, fc, 512 * half:512 * (half + 1)],
                        start=False, stop=(fc == 7)),
                        inc=("t", 1))
                em.ap(V, lambda e, half=half, tc=tc, rt=rt, g=g, cc=cc: e.tensor_scalar(
                    av.u[:, cc, 512 * half:512 * (half + 1)],
                    rt[:, g, 512 * half:512 * (half + 1)],
                    t_.ln_rstd[:, tc, 0:1], t_.ln_rstd[:, tc, 1:2],
                    ALU.mult, op1=ALU.add),
                    waits=[("s", em.count("s"))] + ([("od", 0)] if wout_hook else []),
                    dr=True)
                vfin = em.ap(V, lambda e, half=half, cc=cc, pp=pp: e.tensor_add(
                    av.u[:, cc, 512 * half:512 * (half + 1)],
                    av.u[:, cc, 512 * half:512 * (half + 1)], pp),
                    inc=("v", 1), waits=[("t", tmm)], dr=True)
            if wout_hook:
                wout_hook(em, tc, cc, vfin)
            em.ap(V, lambda e, cc=cc, tc=tc: e.bn_stats(
                t_.ln_stats[:, tc, 0:6], av.u[:, cc, 0:512]), dr=True)
            em.ap(V, lambda e, cc=cc, tc=tc: e.bn_stats(
                t_.ln_stats[:, tc, 6:12], av.u[:, cc, 512:1024]))
            va = em.ap(V, lambda e, tc=tc: e.bn_aggr(
                t_.ln2_mv[:, tc, :], t_.ln_stats[:, tc, :]), inc=("v", 1), dr=True)
            sq = em.ap(S, lambda e, tc=tc: e.activation(
                t_.ln2_mv[:, tc, 1:2], t_.ln2_mv[:, tc, 1:2], ACT.Sqrt, bias=LN_EPS),
                inc=("s", 1), waits=[("v", va)])
            em.ap(V, lambda e, tc=tc: e.reciprocal(
                t_.ln2_mv[:, tc, 1:2], t_.ln2_mv[:, tc, 1:2]),
                waits=[("s", sq)], dr=True)
            vmn = em.ap(V, lambda e, tc=tc: e.tensor_mul(
                t_.ln2_mv[:, tc, 0:1], t_.ln2_mv[:, tc, 0:1], t_.ln2_mv[:, tc, 1:2]),
                inc=("v", 1), dr=True)
            vmn = em.ap(V, lambda e, tc=tc: e.tensor_scalar_mul(
                t_.ln2_mv[:, tc, 0:1], t_.ln2_mv[:, tc, 0:1], -1.0), inc=("v", 1), dr=True)
            sfin = em.ap(S, lambda e, cc=cc, tc=tc: e.activation(
                t_.pool_e[:, 0:1024], av.u[:, cc, :], ACT.Identity,
                scale=t_.ln2_mv[:, tc, 1:2], bias=t_.ln2_mv[:, tc, 0:1]),
                inc=("s", 1), waits=[("v", vmn)])
            em.ap(V, lambda e: e.tensor_mul(
                t_.pool_e[:, 0:1024], t_.pool_e[:, 0:1024], t_.g_rep[:, :]),
                waits=[("s", sfin), ("g", gbc)], dr=True)
            em.ap(V, lambda e: e.tensor_add(
                t_.pool_e[:, 0:1024], t_.pool_e[:, 0:1024], t_.b_rep[:, :]), dr=True)
            vfin = em.ap(V, lambda e, rt=rt, g=g: e.tensor_add(
                rt[:, g, :], rt[:, g, :], t_.pool_e[:, 0:1024]),
                inc=("v", 1), dr=True)
    return ("v", vfin)



def build_full():
    nc = bass.Bass("TRN2", target_bir_lowering=False, detect_race_conditions=False)
    io = alloc_io_ffn(nc)
    ioa = alloc_io_attn(nc)
    out_dram = nc.dram_tensor("x_out", [T, D], F32, kind="ExternalOutput").ap()
    stack = ExitStack()
    with stack:
        em = Emitter(nc)
        sb = make_sb(nc, stack)
        cs = alloc_consts(nc, stack, em)
        t_ = alloc_selection(nc, stack, cs, None)
        alloc_ffn(nc, stack, cs, t_)
        alloc_attn(nc, stack, cs, t_)
        av = attn_views(t_)
        resid_e = sb("resid_e", [128, 8, D], F32)
        resid_o = sb("resid_o", [128, 8, D], F32)
        resid_pair = (resid_e, resid_o)
        emit_attn_w_staging(em, t_, ioa)
        emit_resid_load(em, t_, io, resid_pair)
        emit_ffn_bias_loads(em, t_, io)
        emit_gating_weight_loads(em, t_, io)
        em.ap("vector", lambda e: e.memset(t_.zeros_scan[:, :, :], 0.0))
        done1 = emit_ffn_call(em, t_, io, None, resid_pair, 0, ready=[("cd", 0)])
        done_a = emit_attn(em, t_, ioa, av, resid_pair, done1)
        done2 = emit_ffn_call(em, t_, io, None, resid_pair, 1, ready=[done_a])
        fin = emit_resid_store(em, t_, out_dram, resid_pair, [done2])
        em.wait("sync", fin[0], fin[1])
        with nc.Block() as block:
            em.install(block)
        lower_extended_insts(nc)
    return nc, cs._inputs


def kernel(**inputs):
    nc, cin = build_full()
    x = np.asarray(inputs["L0_attn_x"], dtype=np.float32)
    f32 = lambda k: np.ascontiguousarray(np.asarray(inputs[k], dtype=np.float32))
    base = {
        "ffn_Wg": f32("L0_ffn_Wg"),
        "ffn_bg": f32("L0_ffn_bg").reshape(E, 1),
        "ffn_w1": f32("L0_ffn_w1").reshape(E * D, D),
        "ffn_w2": f32("L0_ffn_w2").reshape(E * D, D),
        "ffn_w3": f32("L0_ffn_w3").reshape(E * D, D),
        "ffn_b1": f32("L0_ffn_b1").reshape(E, 1),
        "ffn_b2": f32("L0_ffn_b2").reshape(E, 1),
        "ffn_b3": f32("L0_ffn_b3").reshape(E, 1),
        "attn_Wq": f32("L0_attn_Wq"), "attn_Wk": f32("L0_attn_Wk"),
        "attn_Wv": f32("L0_attn_Wv"), "attn_Wout": f32("L0_attn_Wout"),
        "attn_Wbeta": f32("L0_attn_Wbeta"), "attn_Wmix": f32("L0_attn_Wmix"),
        "attn_Wgate": f32("L0_attn_Wgate"),
        "attn_bbeta": f32("L0_attn_bbeta").reshape(16, 1),
        "attn_bmix": f32("L0_attn_bmix").reshape(16, 1),
        "attn_bgate": f32("L0_attn_bgate").reshape(8, 1),
        "attn_bout": f32("L0_attn_bout").reshape(1, D),
        "attn_ln_g": f32("L0_attn_ln_g").reshape(1, D),
        "attn_ln_b": f32("L0_attn_ln_b").reshape(1, D),
        **cin,
    }
    in_maps = [dict(base, x_in=np.ascontiguousarray(x[b])) for b in range(8)]
    res = run_bass_kernel_spmd(nc, in_maps, core_ids=list(range(8)))
    out = np.stack([np.asarray(res.results[b]["x_out"]) for b in range(8)], axis=0)
    return out.astype(np.float32)
